# version 03
# baseline (speedup 1.0000x reference)
import numpy as np
import ml_dtypes
from contextlib import ExitStack

import concourse.bass as bass
import concourse.mybir as mybir
from concourse.bass_utils import run_bass_kernel_spmd

F32 = mybir.dt.float32
BF16 = mybir.dt.bfloat16
AF = mybir.ActivationFunctionType
ALU = mybir.AluOpType
AX = mybir.AxisListType

P = 128
D = 1024
T = 2048
KD = D // P
NTT = T // P
DS = 16
RK = 64
NE = 16
ED = 512
NET = ED // P
TQ = T // 512
TCC = 512
NCH = T // TCC
NTC = TCC // P

DEBUG = False


class Ctr:
    def __init__(self, stk, nc, name):
        self.h = stk.enter_context(nc.semaphore(name))
        self.v = 0

    def bump(self, k=1):
        self.v += k
        return self.v


class Sched:
    ENGS = ("pe", "act", "dve", "pool", "sp")

    def __init__(self):
        self.q = {e: [] for e in self.ENGS}

    def emit(self, eng, fn):
        self.q[eng].append(fn)

    def run_block(self, nc):
        qq = self.q
        with nc.Block() as blk:
            @blk.tensor
            def _(e):
                for f in qq["pe"]:
                    f(e)

            @blk.scalar
            def _(e):
                for f in qq["act"]:
                    f(e)

            @blk.vector
            def _(e):
                for f in qq["dve"]:
                    f(e)

            @blk.gpsimd
            def _(e):
                for f in qq["pool"]:
                    f(e)

            @blk.sync
            def _(e):
                for f in qq["sp"]:
                    f(e)
        self.q = {e: [] for e in self.ENGS}


def build_nc():
    nc = bass.Bass("TRN2", target_bir_lowering=False, detect_race_conditions=False)

    x_d = nc.dram_tensor("x", [T, D], F32, kind="ExternalInput").ap()
    alog_d = nc.dram_tensor("A_log", [D, DS], F32, kind="ExternalInput").ap()
    wx_d = nc.dram_tensor("W_x", [RK + DS, D], F32, kind="ExternalInput").ap()
    wdt_d = nc.dram_tensor("W_dt", [D, RK], F32, kind="ExternalInput").ap()
    bdt_d = nc.dram_tensor("b_dt", [D], F32, kind="ExternalInput").ap()
    wc_d = nc.dram_tensor("W_C", [DS, D], F32, kind="ExternalInput").ap()
    rw_d = nc.dram_tensor("router_w", [NE, D], F32, kind="ExternalInput").ap()
    rb_d = nc.dram_tensor("router_bias", [NE], F32, kind="ExternalInput").ap()
    up_d = nc.dram_tensor("up_proj", [NE, D, ED], F32, kind="ExternalInput").ap()
    gate_d = nc.dram_tensor("gate_proj", [NE, D, ED], F32, kind="ExternalInput").ap()
    down_d = nc.dram_tensor("down_proj", [NE, ED, D], F32, kind="ExternalInput").ap()
    shg_d = nc.dram_tensor("sh_gate", [D, D], F32, kind="ExternalInput").ap()
    shu_d = nc.dram_tensor("sh_up", [D, D], F32, kind="ExternalInput").ap()
    shd_d = nc.dram_tensor("sh_down", [D, D], F32, kind="ExternalInput").ap()
    idf_d = nc.dram_tensor("identf", [P, P], F32, kind="ExternalInput").ap()
    idb_d = nc.dram_tensor("identb", [P, P], BF16, kind="ExternalInput").ap()
    iota_d = nc.dram_tensor("iota16", [P, NE], F32, kind="ExternalInput").ap()
    iota100_d = nc.dram_tensor("iota100", [P, NE], F32, kind="ExternalInput").ap()
    e16_d = nc.dram_tensor("e16", [NE, NE * P], F32, kind="ExternalInput").ap()
    out_d = nc.dram_tensor("out", [T, D], F32, kind="ExternalOutput").ap()
    if DEBUG:
        dxn_d = nc.dram_tensor("d_xnT", [P, KD * T], F32, kind="ExternalOutput").ap()
        dcmb_d = nc.dram_tensor("d_cmb", [NE, T], F32, kind="ExternalOutput").ap()
        dacc_d = nc.dram_tensor("d_acc", [P, KD * T], F32, kind="ExternalOutput").ap()
        ddel_d = nc.dram_tensor("d_del", [P, KD * TCC], F32, kind="ExternalOutput").ap()
        dy_d = nc.dram_tensor("d_y", [P, TCC], F32, kind="ExternalOutput").ap()

    with ExitStack() as stk:
        sb = lambda n_, s_, d_: stk.enter_context(nc.sbuf_tensor(n_, s_, d_))
        xnT = sb("xnT", [P, KD * T], BF16)
        accb = sb("accb", [P, KD * T], BF16)
        rstdT = sb("rstdT", [P, NTT], F32)
        negmrT = sb("negmrT", [P, NTT], F32)
        Ab = sb("Ab", [P, KD * DS], F32)
        WxTb = sb("WxTb", [P, KD * (RK + DS)], BF16)
        WdtT = sb("WdtT", [RK, KD * P], BF16)
        WCT = sb("WCT", [P, KD * DS], BF16)
        bdtb = sb("bdtb", [P, KD], F32)
        idf = sb("idf", [P, P], F32)
        idb = sb("idb", [P, P], BF16)
        iota = sb("iota", [P, NE], F32)
        iota100 = sb("iota100s", [P, NE], F32)
        e16b = sb("e16b", [NE, NE * P], BF16)
        hcarry = sb("hcarry", [P, KD * DS], F32)

        PE = Ctr(stk, nc, "PEs")
        AE = Ctr(stk, nc, "AEs")
        VE = Ctr(stk, nc, "VEs")
        PL = Ctr(stk, nc, "PLs")
        SPD = Ctr(stk, nc, "SPDd")
        GPD = Ctr(stk, nc, "GPDd")
        OUT = Ctr(stk, nc, "OUTd")
        DBG = Ctr(stk, nc, "DBGd")

        S = Sched()

        with ExitStack() as pab:
            cmb = pab.enter_context(nc.sbuf_tensor("cmb", [NE, T], F32))
            e16s = pab.enter_context(nc.sbuf_tensor("e16s", [NE, NE * P], F32))
            with ExitStack() as psh:
                sbh = lambda n_, s_, d_: psh.enter_context(nc.sbuf_tensor(n_, s_, d_))
                shgT = sbh("shgT", [P, KD * D], BF16)
                shuT = sbh("shuT", [P, KD * D], BF16)
                shdT = sbh("shdT", [P, KD * D], BF16)

                with ExitStack() as pa:
                    sba = lambda n_, s_, d_: pa.enter_context(nc.sbuf_tensor(n_, s_, d_))
                    wxn = sba("wxn", [RK + DS, D], BF16)
                    wdtn = sba("wdtn", [P, KD * RK], BF16)
                    wcn = sba("wcn", [NE, D], BF16)
                    rwn = sba("rwn", [NE, D], F32)
                    alogn = sba("alogn", [P, KD * DS], F32)
                    bdt8 = sba("bdt8", [KD, P], F32)
                    shnat = sba("shnat", [P, KD * D], BF16)
                    rbs = sba("rbs", [NE, 1], F32)
                    xt = [sba(f"xt{i}", [P, D], F32) for i in range(3)]
                    xsq = [sba(f"xsq{i}", [P, D], F32) for i in range(1)]
                    xnbt = [sba(f"xnbt{i}", [P, D], BF16) for i in range(2)]
                    xnft = [sba(f"xnft{i}", [P, D], F32) for i in range(1)]
                    xfs8 = sba("xfs8", [P, D], F32)
                    lgT = sba("lgT", [NE, T], F32)
                    lgtm = sba("lgtm", [P, NTT * NE], F32)
                    st1 = {nm: sba(nm, [P, NTT], F32)
                           for nm in ["s1c", "s2c", "mc", "qc", "vc", "m1c", "i1c",
                                      "m2c", "i2c", "w1c", "w2c", "mrc"]}
                    scr16 = [sba(f"scr16_{i}", [P, NTT * NE], F32) for i in range(2)]
                    scr16.append(None)
                    ctm = sba("ctm", [P, NTT * NE], F32)

                    pLog = pa.enter_context(nc.psum_tensor("pLog", [NE, T], F32))
                    ptb8 = [pa.enter_context(nc.psum_tensor(f"ptb8{i}", [P, 1024], BF16))
                            for i in range(2)]
                    ptf8 = pa.enter_context(nc.psum_tensor("ptf8", [P, 1024], F32))

                    SPD.bump(6 * 16)
                    consts_at = SPD.v

                    def a1_sp(s):
                        for dst, src in [(idf, idf_d), (idb, idb_d), (iota, iota_d),
                                         (iota100, iota100_d), (e16s, e16_d)]:
                            s.dma_start(dst.ap(), src).then_inc(SPD.h, 16)
                        s.dma_start(rbs.ap(), rb_d.unsqueeze(1)
                                    ).then_inc(SPD.h, 16)
                    S.emit("sp", a1_sp)

                    GPD.bump(8 * 16)
                    w_loads = GPD.v

                    def a1_gp(g):
                        g.dma_start(e16b.ap(), e16_d).then_inc(GPD.h, 16)
                        g.dma_start(wxn.ap(), wx_d).then_inc(GPD.h, 16)
                        g.dma_start(wdtn.ap().rearrange("p (c r) -> p c r", c=KD),
                                    wdt_d.rearrange("(c p) r -> p c r", p=P)
                                    ).then_inc(GPD.h, 16)
                        g.dma_start(wcn.ap(), wc_d).then_inc(GPD.h, 16)
                        g.dma_start(rwn.ap(), rw_d).then_inc(GPD.h, 16)
                        g.dma_start(alogn.ap().rearrange("p (c n) -> p c n", c=KD),
                                    alog_d.rearrange("(c p) n -> p c n", p=P)
                                    ).then_inc(GPD.h, 16)
                        g.dma_start(bdt8.ap(),
                                    bdt_d.rearrange("(c p) -> c p", p=P)
                                    ).then_inc(GPD.h, 16)
                        g.dma_start(shnat.ap().rearrange("p (c d) -> p c d", c=KD),
                                    shg_d.rearrange("(c p) d -> p c d", p=P)
                                    ).then_inc(GPD.h, 16)
                    S.emit("pool", a1_gp)

                    rwT = sba("rwT", [P, KD * NE], F32)
                    wt_groups = []
                    wt_groups.append(("b0",
                        [(wxn.ap()[:, j * P:(j + 1) * P], j * 80) for j in range(KD)],
                        [(WxTb.ap(), P, 0, KD * 80)]))
                    wt_groups.append(("b1",
                        [(wdtn.ap()[:, j * RK:(j + 1) * RK], j * P) for j in range(KD)],
                        [(WdtT.ap(), RK, 0, KD * P)]))
                    wt_groups.append(("b0",
                        [(wcn.ap()[:, j * P:(j + 1) * P], j * DS) for j in range(KD)],
                        [(WCT.ap(), P, 0, KD * DS)]))
                    wt_groups.append(("f",
                        [(rwn.ap()[:, j * P:(j + 1) * P], j * NE) for j in range(KD)]
                        + [(bdt8.ap(), KD * NE)],
                        [(rwT.ap(), P, 0, KD * NE), (bdtb.ap(), P, KD * NE, KD * NE + KD)]))

                    wt_pe = {}
                    wt_ve = {}
                    for gi, (kind, srcs, dsts) in enumerate(wt_groups):
                        wt_pe[gi] = PE.bump(len(srcs))
                    for gi, (kind, srcs, dsts) in enumerate(wt_groups):
                        wt_ve[gi] = VE.bump(len(dsts))

                    def wt_ps(kind):
                        return {"b0": ptb8[0], "b1": ptb8[1], "f": ptf8}[kind]

                    def a2_pe(t):
                        t.wait_ge(GPD.h, w_loads)
                        t.wait_ge(SPD.h, consts_at)
                        seen = {}
                        for gi, (kind, srcs, dsts) in enumerate(wt_groups):
                            if kind in seen:
                                t.wait_ge(VE.h, wt_ve[seen[kind]])
                            seen[kind] = gi
                            ps = wt_ps(kind)
                            for si, (sap, coff) in enumerate(srcs):
                                npart, nfree = sap.shape
                                ident = idb if kind != "f" else idf
                                t.transpose(ps.ap()[0:nfree, coff:coff + npart], sap,
                                            ident.ap()[0:npart, 0:npart]
                                            ).then_inc(PE.h, 1)
                    S.emit("pe", a2_pe)

                    def a2_dve(v):
                        for gi, (kind, srcs, dsts) in enumerate(wt_groups):
                            ps = wt_ps(kind)
                            v.wait_ge(PE.h, wt_pe[gi])
                            for dap, rows, lo, hi in dsts:
                                v.tensor_copy(dap, ps.ap()[0:rows, lo:hi]
                                              ).then_inc(VE.h, 1)
                    S.emit("dve", a2_dve)

                    aexp_at = AE.bump()

                    def a2_act(a):
                        a.wait_ge(GPD.h, w_loads)
                        a.activation(Ab.ap(), alogn.ap(), AF.Exp).then_inc(AE.h, 1)
                    S.emit("act", a2_act)

                    aneg_at = VE.bump()

                    def a2_dve2(v):
                        v.wait_ge(AE.h, aexp_at)
                        v.tensor_scalar(Ab.ap(), Ab.ap(), -1.0, None, ALU.mult
                                        ).then_inc(VE.h, 1)
                    S.emit("dve", a2_dve2)

                    A3 = {tau: {} for tau in range(NTT)}
                    for tau in range(NTT):
                        A3[tau]["x"] = SPD.bump(16)
                    for tau in range(NTT):
                        d = A3[tau]
                        d["sq"] = AE.bump()
                        d["ln"] = AE.bump()
                        d["rstd"] = AE.bump()
                        d["xnb"] = AE.bump()
                        d["xnf"] = AE.bump()
                    for tau in range(NTT):
                        d = A3[tau]
                        d["s1"] = VE.bump()
                        d["s2"] = VE.bump()
                        d["m"] = VE.bump()
                        d["q"] = VE.bump()
                        d["v1"] = VE.bump()
                        d["v2"] = VE.bump()
                        d["stats"] = VE.bump()
                        d["mr1"] = VE.bump()
                        d["mr"] = VE.bump()
                        d["bevac"] = VE.bump()
                        d["fevac"] = VE.bump()
                    for tau in range(NTT):
                        d = A3[tau]
                        d["btr"] = PE.bump(KD)
                        d["ftr"] = PE.bump(KD)
                        d["lmm"] = [PE.bump() for _ in range(KD)]

                    def a3_sp(s):
                        for tau in range(NTT):
                            if tau >= 3:
                                s.wait_ge(AE.h, A3[tau - 3]["xnf"])
                                s.wait_ge(VE.h, A3[tau - 3]["stats"])
                            s.dma_start(xt[tau % 3].ap(),
                                        x_d[tau * P:(tau + 1) * P, :]
                                        ).then_inc(SPD.h, 16)
                    S.emit("sp", a3_sp)

                    def a3_act(a):
                        for tau in range(NTT):
                            xa = xt[tau % 3].ap()
                            a.wait_ge(SPD.h, A3[tau]["x"])
                            if tau >= 1:
                                a.wait_ge(VE.h, A3[tau - 1]["s2"])
                            a.activation(xsq[0].ap(), xa, AF.Square
                                         ).then_inc(AE.h, 1)
                            a.wait_ge(VE.h, A3[tau]["stats"])
                            vcol = st1["vc"].ap()[:, tau:tau + 1]
                            a.activation(st1["qc"].ap()[:, tau:tau + 1], vcol, AF.Ln
                                         ).then_inc(AE.h, 1)
                            a.wait_ge(AE.h, A3[tau]["ln"])
                            a.activation(rstdT.ap()[:, tau:tau + 1],
                                         st1["qc"].ap()[:, tau:tau + 1], AF.Exp,
                                         scale=-0.5).then_inc(AE.h, 1)
                            a.wait_ge(VE.h, A3[tau]["mr"])
                            if tau >= 2:
                                a.wait_ge(PE.h, A3[tau - 2]["btr"])
                            if tau >= 1:
                                a.wait_ge(PE.h, A3[tau - 1]["ftr"])
                            a.activation(xnbt[tau % 2].ap(), xa, AF.Identity,
                                         bias=negmrT.ap()[:, tau:tau + 1],
                                         scale=rstdT.ap()[:, tau:tau + 1]
                                         ).then_inc(AE.h, 1)
                            a.activation(xnft[0].ap(), xa, AF.Identity,
                                         bias=negmrT.ap()[:, tau:tau + 1],
                                         scale=rstdT.ap()[:, tau:tau + 1]
                                         ).then_inc(AE.h, 1)
                    S.emit("act", a3_act)

                    def a3_dve(v):
                        inv = 1.0 / D
                        for tau in range(NTT):
                            xa = xt[tau % 3].ap()
                            tc = lambda n: st1[n].ap()[:, tau:tau + 1]
                            d = A3[tau]
                            v.wait_ge(SPD.h, d["x"])
                            v.reduce_sum(tc("s1c"), xa, axis=AX.X).then_inc(VE.h, 1)
                            v.wait_ge(AE.h, d["sq"])
                            v.reduce_sum(tc("s2c"), xsq[0].ap(), axis=AX.X
                                         ).then_inc(VE.h, 1)
                            v.wait_ge(VE.h, d["s1"])
                            v.tensor_scalar(tc("mc"), tc("s1c"), inv, None, ALU.mult
                                            ).then_inc(VE.h, 1)
                            v.wait_ge(VE.h, d["s2"])
                            v.tensor_scalar(tc("qc"), tc("s2c"), inv, None, ALU.mult
                                            ).then_inc(VE.h, 1)
                            v.wait_ge(VE.h, d["m"])
                            v.tensor_tensor(tc("vc"), tc("mc"), tc("mc"), ALU.mult
                                            ).then_inc(VE.h, 1)
                            v.wait_ge(VE.h, max(d["q"], d["v1"]))
                            v.tensor_tensor(tc("vc"), tc("qc"), tc("vc"), ALU.subtract
                                            ).then_inc(VE.h, 1)
                            v.wait_ge(VE.h, d["v2"])
                            v.tensor_scalar(tc("vc"), tc("vc"), 1e-5, None, ALU.add
                                            ).then_inc(VE.h, 1)
                            v.wait_ge(AE.h, d["rstd"])
                            v.tensor_tensor(tc("mrc"), tc("mc"),
                                            rstdT.ap()[:, tau:tau + 1], ALU.mult
                                            ).then_inc(VE.h, 1)
                            v.wait_ge(VE.h, d["mr1"])
                            v.tensor_scalar(negmrT.ap()[:, tau:tau + 1], tc("mrc"),
                                            -1.0, None, ALU.mult).then_inc(VE.h, 1)
                            v.wait_ge(PE.h, d["btr"])
                            v.tensor_copy(
                                xnT.ap().rearrange("p (j t2) -> p j t2", t2=T)[
                                    :, :, tau * P:(tau + 1) * P],
                                ptb8[tau % 2].ap().rearrange(
                                    "p (j t2) -> p j t2", t2=P)
                            ).then_inc(VE.h, 1)
                            v.wait_ge(PE.h, d["ftr"])
                            if tau >= 1:
                                v.wait_ge(PE.h, A3[tau - 1]["lmm"][KD - 1])
                            v.tensor_copy(xfs8.ap(), ptf8.ap()).then_inc(VE.h, 1)
                    S.emit("dve", a3_dve)

                    def a3_pe(t):
                        for tau in range(NTT):
                            d = A3[tau]
                            t.wait_ge(AE.h, d["xnb"])
                            if tau >= 2:
                                t.wait_ge(VE.h, A3[tau - 2]["bevac"])
                            for j in range(KD):
                                t.transpose(ptb8[tau % 2].ap()[:, j * P:(j + 1) * P],
                                            xnbt[tau % 2].ap()[:, j * P:(j + 1) * P],
                                            idb.ap()).then_inc(PE.h, 1)
                            t.wait_ge(AE.h, d["xnf"])
                            if tau >= 1:
                                t.wait_ge(VE.h, A3[tau - 1]["fevac"])
                            for j in range(KD):
                                t.transpose(ptf8.ap()[:, j * P:(j + 1) * P],
                                            xnft[0].ap()[:, j * P:(j + 1) * P],
                                            idf.ap()).then_inc(PE.h, 1)
                            t.wait_ge(VE.h, d["fevac"])
                            for j in range(KD):
                                t.matmul(pLog.ap()[:, tau * P:(tau + 1) * P],
                                         rwT.ap()[:, j * NE:(j + 1) * NE],
                                         xfs8.ap()[:, j * P:(j + 1) * P],
                                         start=(j == 0), stop=(j == KD - 1)
                                         ).then_inc(PE.h, 1)
                    S.emit("pe", a3_pe)

                    lgb_at = AE.bump()

                    def a4_act(a):
                        a.wait_ge(PE.h, A3[NTT - 1]["lmm"][KD - 1])
                        a.activation(lgT.ap(), pLog.ap(), AF.Identity, bias=rbs.ap()
                                     ).then_inc(AE.h, 1)
                    S.emit("act", a4_act)

                    lgtr = PE.bump(NTT)
                    lgev = VE.bump()

                    def a4_pe(t):
                        t.wait_ge(AE.h, lgb_at)
                        for tau in range(NTT):
                            t.transpose(ptf8.ap()[:, tau * NE:(tau + 1) * NE],
                                        lgT.ap()[:, tau * P:(tau + 1) * P],
                                        idf.ap()[0:NE, 0:NE]).then_inc(PE.h, 1)
                    S.emit("pe", a4_pe)

                    def a4_dve(v):
                        v.wait_ge(PE.h, lgtr)
                        v.tensor_copy(lgtm.ap(), ptf8.ap()[:, 0:NTT * NE]
                                      ).then_inc(VE.h, 1)
                    S.emit("dve", a4_dve)

                    lgs = lgtm.ap()

                    def batch_step(prereqs, opgen, n_inc=1):
                        at = VE.bump(n_inc)

                        def f(v):
                            for sh, val in prereqs:
                                v.wait_ge(sh, val)
                            last = None
                            for tau in range(NTT):
                                last = opgen(v, tau)
                            last.then_inc(VE.h, n_inc)
                        S.emit("dve", f)
                        return at

                    sl = lambda t_, tau: t_[:, tau * NE:(tau + 1) * NE]
                    col = lambda n_, tau: st1[n_].ap()[:, tau:tau + 1]

                    m1_at = batch_step(
                        [(VE.h, lgev)],
                        lambda v, tau: v.reduce_max(col("m1c", tau),
                                                    sl(lgs, tau), axis=AX.X))
                    eq1_at = batch_step(
                        [(VE.h, m1_at)],
                        lambda v, tau: v.tensor_single_scalar(
                            sl(scr16[0].ap(), tau), sl(lgs, tau),
                            col("m1c", tau), ALU.is_equal))
                    cand1_at = batch_step(
                        [(VE.h, eq1_at)],
                        lambda v, tau: v.scalar_tensor_tensor(
                            sl(scr16[1].ap(), tau), sl(scr16[0].ap(), tau),
                            -100.0, iota100.ap(), ALU.mult, ALU.add))
                    i1_at = batch_step(
                        [(VE.h, cand1_at)],
                        lambda v, tau: v.tensor_reduce(
                            col("i1c", tau), sl(scr16[1].ap(), tau), AX.X, ALU.min))
                    mask1_at = batch_step(
                        [(VE.h, i1_at)],
                        lambda v, tau: v.tensor_single_scalar(
                            sl(scr16[0].ap(), tau), iota.ap(),
                            col("i1c", tau), ALU.is_equal))
                    l2_at = batch_step(
                        [(VE.h, mask1_at)],
                        lambda v, tau: v.scalar_tensor_tensor(
                            sl(scr16[1].ap(), tau), sl(scr16[0].ap(), tau),
                            -1.0e30, sl(lgs, tau), ALU.mult, ALU.add))
                    m2_at = batch_step(
                        [(VE.h, l2_at)],
                        lambda v, tau: v.reduce_max(col("m2c", tau),
                                                    sl(scr16[1].ap(), tau), axis=AX.X))
                    eq2_at = batch_step(
                        [(VE.h, m2_at)],
                        lambda v, tau: v.tensor_single_scalar(
                            sl(ctm.ap(), tau), sl(scr16[1].ap(), tau),
                            col("m2c", tau), ALU.is_equal))
                    cand2_at = batch_step(
                        [(VE.h, eq2_at)],
                        lambda v, tau: v.scalar_tensor_tensor(
                            sl(ctm.ap(), tau), sl(ctm.ap(), tau),
                            -100.0, iota100.ap(), ALU.mult, ALU.add))
                    i2_at = batch_step(
                        [(VE.h, cand2_at)],
                        lambda v, tau: v.tensor_reduce(
                            col("i2c", tau), sl(ctm.ap(), tau), AX.X, ALU.min))

                    dm_at = VE.bump()

                    def s_dm(v):
                        v.wait_ge(VE.h, i2_at)
                        v.tensor_sub(st1["w1c"].ap(), st1["m1c"].ap(),
                                     st1["m2c"].ap()).then_inc(VE.h, 1)
                    S.emit("dve", s_dm)

                    sig_at = AE.bump(2)

                    def s_sig(a):
                        a.wait_ge(VE.h, dm_at)
                        a.activation(st1["w2c"].ap(), st1["w1c"].ap(), AF.Sigmoid,
                                     scale=-1.0).then_inc(AE.h, 1)
                        a.activation(st1["w1c"].ap(), st1["w1c"].ap(), AF.Sigmoid
                                     ).then_inc(AE.h, 1)
                    S.emit("act", s_sig)

                    mask2_at = batch_step(
                        [(AE.h, sig_at)],
                        lambda v, tau: v.tensor_single_scalar(
                            sl(scr16[1].ap(), tau), iota.ap(),
                            col("i2c", tau), ALU.is_equal))
                    cw_at = VE.bump()

                    def s_cw(v):
                        v.wait_ge(VE.h, mask2_at)
                        for tau in range(NTT):
                            v.tensor_scalar(sl(ctm.ap(), tau), sl(scr16[0].ap(), tau),
                                            col("w1c", tau), None, ALU.mult)
                        last = None
                        for tau in range(NTT):
                            last = v.tensor_scalar(sl(scr16[1].ap(), tau),
                                                   sl(scr16[1].ap(), tau),
                                                   col("w2c", tau), None, ALU.mult)
                        last.then_inc(VE.h, 1)
                    S.emit("dve", s_cw)

                    comb_at = batch_step(
                        [(VE.h, cw_at)],
                        lambda v, tau: v.tensor_add(sl(ctm.ap(), tau),
                                                    sl(ctm.ap(), tau),
                                                    sl(scr16[1].ap(), tau)))

                    ctr_at = PE.bump(NTT)
                    cev_at = VE.bump()

                    def a4_pe2(t):
                        t.wait_ge(VE.h, comb_at)
                        for tau in range(NTT):
                            t.transpose(pLog.ap()[0:NE, tau * P:(tau + 1) * P],
                                        ctm.ap()[:, tau * NE:(tau + 1) * NE],
                                        idf.ap()).then_inc(PE.h, 1)
                    S.emit("pe", a4_pe2)

                    def a4_dve2(v):
                        v.wait_ge(PE.h, ctr_at)
                        v.tensor_copy(cmb.ap(), pLog.ap()[0:NE, :]
                                      ).then_inc(VE.h, 1)
                    S.emit("dve", a4_dve2)

                    sh_pe = {}
                    sh_ve = {}
                    for w in range(3):
                        for dc in range(KD):
                            sh_pe[(w, dc)] = PE.bump(KD)
                    for w in range(3):
                        for dc in range(KD):
                            sh_ve[(w, dc)] = VE.bump()
                    shu_at = GPD.bump(16)
                    shd_at = GPD.bump(16)

                    def a5_gp(g):
                        g.wait_ge(PE.h, sh_pe[(0, KD - 1)])
                        g.dma_start(shnat.ap().rearrange("p (c d) -> p c d", c=KD),
                                    shu_d.rearrange("(c p) d -> p c d", p=P)
                                    ).then_inc(GPD.h, 16)
                        g.wait_ge(PE.h, sh_pe[(1, KD - 1)])
                        g.dma_start(shnat.ap().rearrange("p (c d) -> p c d", c=KD),
                                    shd_d.rearrange("(c p) d -> p c d", p=P)
                                    ).then_inc(GPD.h, 16)
                    S.emit("pool", a5_gp)

                    def a5_pe(t):
                        for w, dstT in enumerate([shgT, shuT, shdT]):
                            if w == 1:
                                t.wait_ge(GPD.h, shu_at)
                            if w == 2:
                                t.wait_ge(GPD.h, shd_at)
                            for dc in range(KD):
                                lin = w * KD + dc
                                if lin >= 2:
                                    pw, pdc = divmod(lin - 2, KD)
                                    t.wait_ge(VE.h, sh_ve[(pw, pdc)])
                                for hc in range(KD):
                                    t.transpose(
                                        ptb8[dc % 2].ap()[:, hc * P:(hc + 1) * P],
                                        shnat.ap()[:, hc * D + dc * P:
                                                   hc * D + (dc + 1) * P],
                                        idb.ap()).then_inc(PE.h, 1)
                    S.emit("pe", a5_pe)

                    def a5_dve(v):
                        for w, dstT in enumerate([shgT, shuT, shdT]):
                            for dc in range(KD):
                                v.wait_ge(PE.h, sh_pe[(w, dc)])
                                v.tensor_copy(
                                    dstT.ap()[:, dc * D:(dc + 1) * D],
                                    ptb8[dc % 2].ap()).then_inc(VE.h, 1)
                    S.emit("dve", a5_dve)

                    if DEBUG:
                        DBG.bump(32)
                        vnow = VE.v

                        def dbg_a(g):
                            g.wait_ge(VE.h, vnow)
                            g.dma_start(dcmb_d, cmb.ap()).then_inc(DBG.h, 16)
                            g.dma_start(dxn_d, xnT.ap()).then_inc(DBG.h, 16)
                            g.wait_ge(DBG.h, 32)
                        S.emit("pool", dbg_a)

                    S.run_block(nc)

                with ExitStack() as pb1:
                    sbb1 = lambda n_, s_, d_: pb1.enter_context(
                        nc.sbuf_tensor(n_, s_, d_))
                    shh = sbb1("shh", [P, KD * T], BF16)
                    sigb = [sbb1(f"sigb{i}", [P, 1024], BF16) for i in range(2)]
                    PS1 = pb1.enter_context(nc.psum_tensor("PS1", [P, 1024], F32))
                    PS2 = pb1.enter_context(nc.psum_tensor("PS2", [P, 1024], F32))
                    PS3 = [pb1.enter_context(nc.psum_tensor(f"PS3{i}", [P, 1024], F32))
                           for i in range(2)]

                    glu = {}
                    for th in range(2):
                        for ht in range(KD):
                            glu[("g", th, ht)] = PE.bump(16)
                            glu[("u", th, ht)] = PE.bump(16)
                        for dt in range(KD):
                            glu[("d", th, dt)] = PE.bump(16)
                    for th in range(2):
                        for ht in range(KD):
                            glu[("sig", th, ht)] = AE.bump()
                    for th in range(2):
                        for ht in range(KD):
                            glu[("shh", th, ht)] = VE.bump()
                        for dt in range(KD):
                            glu[("acc", th, dt)] = VE.bump()

                    def prev_ht(th, ht, k):
                        lin = th * KD + ht - k
                        return (lin // KD, lin % KD) if lin >= 0 else None

                    def b1_pe(t):
                        for th in range(2):
                            for ht in range(KD):
                                pv = prev_ht(th, ht, 1)
                                if pv is not None:
                                    t.wait_ge(AE.h, glu[("sig",) + pv])
                                for dc in range(KD):
                                    for q in range(2):
                                        t.matmul(
                                            PS1.ap()[:, q * 512:(q + 1) * 512],
                                            shgT.ap()[:, dc * D + ht * P:
                                                      dc * D + (ht + 1) * P],
                                            xnT.ap()[:, dc * T + th * 1024 + q * 512:
                                                     dc * T + th * 1024 + (q + 1) * 512],
                                            start=(dc == 0), stop=(dc == KD - 1)
                                        ).then_inc(PE.h, 1)
                                if pv is not None:
                                    t.wait_ge(VE.h, glu[("shh",) + pv])
                                for dc in range(KD):
                                    for q in range(2):
                                        t.matmul(
                                            PS2.ap()[:, q * 512:(q + 1) * 512],
                                            shuT.ap()[:, dc * D + ht * P:
                                                      dc * D + (ht + 1) * P],
                                            xnT.ap()[:, dc * T + th * 1024 + q * 512:
                                                     dc * T + th * 1024 + (q + 1) * 512],
                                            start=(dc == 0), stop=(dc == KD - 1)
                                        ).then_inc(PE.h, 1)
                            t.wait_ge(VE.h, glu[("shh", th, KD - 1)])
                            for dt in range(KD):
                                lin = th * KD + dt
                                if lin >= 2:
                                    pv = ((lin - 2) // KD, (lin - 2) % KD)
                                    t.wait_ge(VE.h, glu[("acc",) + pv])
                                for hc in range(KD):
                                    for q in range(2):
                                        t.matmul(
                                            PS3[dt % 2].ap()[:, q * 512:(q + 1) * 512],
                                            shdT.ap()[:, hc * D + dt * P:
                                                      hc * D + (dt + 1) * P],
                                            shh.ap()[:, hc * T + th * 1024 + q * 512:
                                                     hc * T + th * 1024 + (q + 1) * 512],
                                            start=(hc == 0), stop=(hc == KD - 1)
                                        ).then_inc(PE.h, 1)
                    S.emit("pe", b1_pe)

                    def b1_act(a):
                        for th in range(2):
                            for ht in range(KD):
                                a.wait_ge(PE.h, glu[("g", th, ht)])
                                pv = prev_ht(th, ht, 2)
                                if pv is not None:
                                    a.wait_ge(VE.h, glu[("shh",) + pv])
                                a.activation(sigb[ht % 2].ap(), PS1.ap(), AF.Silu
                                             ).then_inc(AE.h, 1)
                    S.emit("act", b1_act)

                    def b1_dve(v):
                        for th in range(2):
                            for ht in range(KD):
                                v.wait_ge(PE.h, glu[("u", th, ht)])
                                v.wait_ge(AE.h, glu[("sig", th, ht)])
                                v.tensor_tensor(
                                    shh.ap()[:, ht * T + th * 1024:
                                             ht * T + (th + 1) * 1024],
                                    PS2.ap(), sigb[ht % 2].ap(), ALU.mult
                                ).then_inc(VE.h, 1)
                            for dt in range(KD):
                                v.wait_ge(PE.h, glu[("d", th, dt)])
                                v.tensor_copy(
                                    accb.ap()[:, dt * T + th * 1024:
                                              dt * T + (th + 1) * 1024],
                                    PS3[dt % 2].ap()).then_inc(VE.h, 1)
                    S.emit("dve", b1_dve)

                    S.run_block(nc)

            gb = accb
            with ExitStack() as pb2:
                sbb = lambda n_, s_, d_: pb2.enter_context(nc.sbuf_tensor(n_, s_, d_))
                upw = [sbb(f"upw{i}", [P, KD * ED], BF16) for i in range(2)]
                gatew = [sbb(f"gatew{i}", [P, KD * ED], BF16) for i in range(2)]
                downw = [sbb(f"downw{i}", [P, NET * D], BF16) for i in range(2)]
                hid = sbb("hid", [P, NET * T], BF16)
                cbf = sbb("cbf", [P, T], F32)
                sigb2 = [sbb(f"sigb2{i}", [P, T], BF16) for i in range(2)]
                upcb = [sbb(f"upcb{i}", [P, T], BF16) for i in range(2)]
                PA = pb2.enter_context(nc.psum_tensor("PA", [P, T], F32))
                PB = pb2.enter_context(nc.psum_tensor("PB", [P, T], F32))

                EX = {}
                for k in range(NE):
                    EX[("w", k)] = GPD.bump(3 * 16)
                    EX[("cb", k)] = PE.bump(TQ)
                    for et in range(NET):
                        EX[("up", k, et)] = PE.bump(32)
                        EX[("gate", k, et)] = PE.bump(32)
                    for dt in range(KD):
                        EX[("down", k, dt)] = PE.bump(16)
                for k in range(NE):
                    for et in range(NET):
                        EX[("sig", k, et)] = AE.bump()
                for k in range(NE):
                    EX[("cbe", k)] = VE.bump()
                    for et in range(NET):
                        EX[("upc", k, et)] = VE.bump()
                        EX[("hid", k, et)] = VE.bump()
                    for dt in range(KD):
                        EX[("acc", k, dt)] = VE.bump()

                def b2_pool(g):
                    for k in range(NE):
                        if k >= 2:
                            g.wait_ge(PE.h, EX[("down", k - 2, KD - 1)])
                        b = k % 2
                        g.dma_start(upw[b].ap().rearrange("p (c e) -> p c e", c=KD),
                                    up_d[k].rearrange("(c p) e -> p c e", p=P)
                                    ).then_inc(GPD.h, 16)
                        g.dma_start(gatew[b].ap().rearrange("p (c e) -> p c e", c=KD),
                                    gate_d[k].rearrange("(c p) e -> p c e", p=P)
                                    ).then_inc(GPD.h, 16)
                        g.dma_start(downw[b].ap().rearrange("p (c d) -> p c d", c=NET),
                                    down_d[k].rearrange("(c p) d -> p c d", p=P)
                                    ).then_inc(GPD.h, 16)
                S.emit("pool", b2_pool)

                def b2_pe(t):
                    for k in range(NE):
                        b = k % 2
                        t.wait_ge(GPD.h, EX[("w", k)])
                        if k >= 1:
                            t.wait_ge(VE.h, EX[("acc", k - 1, KD - 2)])
                        for q in range(TQ):
                            t.matmul(PA.ap()[:, q * 512:(q + 1) * 512],
                                     e16s.ap()[:, k * P:(k + 1) * P],
                                     cmb.ap()[:, q * 512:(q + 1) * 512],
                                     start=True, stop=True).then_inc(PE.h, 1)
                        t.wait_ge(VE.h, EX[("cbe", k)])
                        for et in range(NET):
                            if et >= 1:
                                t.wait_ge(VE.h, EX[("upc", k, et - 1)])
                            for kc in range(KD):
                                for q in range(TQ):
                                    t.matmul(PA.ap()[:, q * 512:(q + 1) * 512],
                                             upw[b].ap()[:, kc * ED + et * P:
                                                         kc * ED + (et + 1) * P],
                                             xnT.ap()[:, kc * T + q * 512:
                                                      kc * T + (q + 1) * 512],
                                             start=(kc == 0), stop=(kc == KD - 1)
                                             ).then_inc(PE.h, 1)
                            if et >= 1:
                                t.wait_ge(AE.h, EX[("sig", k, et - 1)])
                            elif k >= 1:
                                t.wait_ge(VE.h, EX[("acc", k - 1, KD - 1)])
                            for kc in range(KD):
                                for q in range(TQ):
                                    t.matmul(PB.ap()[:, q * 512:(q + 1) * 512],
                                             gatew[b].ap()[:, kc * ED + et * P:
                                                           kc * ED + (et + 1) * P],
                                             xnT.ap()[:, kc * T + q * 512:
                                                      kc * T + (q + 1) * 512],
                                             start=(kc == 0), stop=(kc == KD - 1)
                                             ).then_inc(PE.h, 1)
                        for dt in range(KD):
                            if dt == 0:
                                t.wait_ge(VE.h, EX[("hid", k, NET - 1)])
                            if dt == 1:
                                t.wait_ge(AE.h, EX[("sig", k, NET - 1)])
                            if dt >= 2:
                                t.wait_ge(VE.h, EX[("acc", k, dt - 2)])
                            PX = PA if dt % 2 == 0 else PB
                            for ec in range(NET):
                                for q in range(TQ):
                                    t.matmul(PX.ap()[:, q * 512:(q + 1) * 512],
                                             downw[b].ap()[:, ec * D + dt * P:
                                                           ec * D + (dt + 1) * P],
                                             hid.ap()[:, ec * T + q * 512:
                                                      ec * T + (q + 1) * 512],
                                             start=(ec == 0), stop=(ec == NET - 1)
                                             ).then_inc(PE.h, 1)
                S.emit("pe", b2_pe)

                def b2_act(a):
                    for k in range(NE):
                        for et in range(NET):
                            a.wait_ge(PE.h, EX[("gate", k, et)])
                            lin = k * NET + et
                            if lin >= 2:
                                pk, pet = divmod(lin - 2, NET)
                                a.wait_ge(VE.h, EX[("hid", pk, pet)])
                            a.activation(sigb2[et % 2].ap(), PB.ap(), AF.Silu
                                         ).then_inc(AE.h, 1)
                S.emit("act", b2_act)

                def b2_dve(v):
                    for k in range(NE):
                        v.wait_ge(PE.h, EX[("cb", k)])
                        v.tensor_copy(cbf.ap(), PA.ap()).then_inc(VE.h, 1)
                        for et in range(NET):
                            v.wait_ge(PE.h, EX[("up", k, et)])
                            lin = k * NET + et
                            if lin >= 2:
                                pk, pet = divmod(lin - 2, NET)
                                v.wait_ge(VE.h, EX[("hid", pk, pet)])
                            v.tensor_tensor(upcb[et % 2].ap(), PA.ap(), cbf.ap(),
                                            ALU.mult).then_inc(VE.h, 1)
                            v.wait_ge(AE.h, EX[("sig", k, et)])
                            v.tensor_tensor(hid.ap()[:, et * T:(et + 1) * T],
                                            sigb2[et % 2].ap(), upcb[et % 2].ap(),
                                            ALU.mult).then_inc(VE.h, 1)
                        for dt in range(KD):
                            v.wait_ge(PE.h, EX[("down", k, dt)])
                            PX = PA if dt % 2 == 0 else PB
                            v.tensor_tensor(accb.ap()[:, dt * T:(dt + 1) * T],
                                            PX.ap(), accb.ap()[:, dt * T:(dt + 1) * T],
                                            ALU.add).then_inc(VE.h, 1)
                S.emit("dve", b2_dve)

                g_at = AE.bump(KD)

                def b2_g(a):
                    a.wait_ge(VE.h, EX[("acc", NE - 1, KD - 1)])
                    for dt in range(KD):
                        a.activation(accb.ap()[:, dt * T:(dt + 1) * T],
                                     accb.ap()[:, dt * T:(dt + 1) * T], AF.Silu
                                     ).then_inc(AE.h, 1)
                S.emit("act", b2_g)

                if DEBUG:
                    DBG.bump(16)

                    def dbg_b(g):
                        g.wait_ge(VE.h, EX[("acc", NE - 1, KD - 1)])
                        g.dma_start(dacc_d, accb.ap()).then_inc(DBG.h, 16)
                        g.wait_ge(DBG.h, 48)
                    S.emit("pool", dbg_b)

                S.run_block(nc)

        with ExitStack() as pc:
            sbc = lambda n_, s_, d_: pc.enter_context(nc.sbuf_tensor(n_, s_, d_))
            xo = sbc("xo", [P, NTC * D], F32)
            delt = sbc("delt", [P, KD * TCC], F32)
            ub1 = sbc("ub1", [P, TCC], BF16)
            yb1 = sbc("yb1", [P, TCC], F32)
            balls = [sbc(f"balls{i}", [P, DS * TCC], BF16) for i in range(2)]
            hall = sbc("hall", [P, DS * TCC], BF16)
            Bball = sbc("Bball", [P, DS * TCC], BF16)
            Cball = sbc("Cball", [P, DS * TCC], BF16)
            Btf = sbc("Btf", [NE, TCC], BF16)
            Ctf = sbc("Ctf", [NE, TCC], BF16)
            dlowb = sbc("dlowb", [RK, TCC], BF16)
            spscr = [sbc(f"spscr{i}", [P, TCC], F32) for i in range(2)]
            abuf = [sbc(f"abuf{i}", [P, TCC], F32) for i in range(2)]
            sgf = [sbc(f"sgf{i}", [P, TCC], F32) for i in range(2)]

            Pg = [pc.enter_context(nc.psum_tensor(f"Pg{i}", [P, 512], F32))
                  for i in range(2)]
            Pd = [pc.enter_context(nc.psum_tensor(f"Pd{i}", [P, 512], F32))
                  for i in range(2)]
            PbcB = pc.enter_context(nc.psum_tensor("PbcB", [NE, 512], F32))
            PbcC = pc.enter_context(nc.psum_tensor("PbcC", [NE, 512], F32))
            Ptf2 = [pc.enter_context(nc.psum_tensor(f"Ptf2{i}", [P, P], F32))
                    for i in range(2)]

            hz_at = VE.bump()

            def c_init(v):
                v.memset(hcarry.ap(), 0.0).then_inc(VE.h, 1)
                v.wait_ge(VE.h, hz_at)
            S.emit("dve", c_init)

            C = {}
            for c in range(NCH):
                C[("x", c)] = SPD.bump(NTC * 16)
                C[("bcB", c)] = PE.bump(KD)
                C[("bcC", c)] = PE.bump(KD)
                C[("dlow", c)] = PE.bump(KD)
                for n in range(DS):
                    C[("bbm", c, n)] = PE.bump()
                for n in range(DS):
                    C[("cbm", c, n)] = PE.bump()
                for dt in range(KD):
                    C[("dpre", c, dt)] = PE.bump()
                for dt in range(KD):
                    for tau in range(NTC):
                        C[("sgt", c, dt, tau)] = PE.bump()
                C[("dlev", c)] = AE.bump()
                for n in range(DS):
                    C[("bbe", c, n)] = AE.bump()
                for n in range(DS):
                    C[("cbe", c, n)] = AE.bump()
                for dt in range(KD):
                    C[("sp1", c, dt)] = AE.bump()
                    C[("sp2", c, dt)] = AE.bump()
                    C[("sp3", c, dt)] = AE.bump()
                for dt in range(KD):
                    for n in range(DS):
                        C[("a", c, dt, n)] = AE.bump()
                C[("Bev", c)] = VE.bump()
                C[("Cev", c)] = VE.bump()
                for dt in range(KD):
                    C[("r", c, dt)] = VE.bump()
                    C[("del", c, dt)] = VE.bump()
                for dt in range(KD):
                    C[("u", c, dt)] = VE.bump()
                    C[("ball", c, dt)] = VE.bump()
                    for n in range(DS):
                        C[("scan", c, dt, n)] = VE.bump()
                    C[("carry", c, dt)] = VE.bump()
                    C[("M", c, dt)] = VE.bump()
                    C[("t1", c, dt)] = VE.bump()
                    C[("t2", c, dt)] = VE.bump()
                    C[("t3", c, dt)] = VE.bump()
                    C[("red", c, dt)] = VE.bump()
                    C[("sg", c, dt)] = VE.bump()
                    for tau in range(NTC):
                        C[("add", c, dt, tau)] = VE.bump()
                C[("out", c)] = OUT.bump(NTC * 16)

            def c_sp(s):
                for c in range(NCH):
                    if c >= 1:
                        s.wait_ge(OUT.h, C[("out", c - 1)])
                    for tau in range(NTC):
                        tg = c * NTC + tau
                        s.dma_start(xo.ap()[:, tau * D:(tau + 1) * D],
                                    x_d[tg * P:(tg + 1) * P, :]).then_inc(SPD.h, 16)
                    for tau in range(NTC):
                        tg = c * NTC + tau
                        s.wait_ge(VE.h, C[("add", c, KD - 1, tau)])
                        s.dma_start(out_d[tg * P:(tg + 1) * P, :],
                                    xo.ap()[:, tau * D:(tau + 1) * D]
                                    ).then_inc(OUT.h, 16)
            S.emit("sp", c_sp)

            def c_act(a):
                for c in range(NCH):
                    a.wait_ge(PE.h, C[("dlow", c)])
                    a.copy(dlowb.ap(), Pd[0].ap()[0:RK, :]).then_inc(AE.h, 1)
                    for n in range(DS):
                        a.wait_ge(PE.h, C[("bbm", c, n)])
                        a.copy(Bball.ap()[:, n * TCC:(n + 1) * TCC],
                               Pg[n % 2].ap()).then_inc(AE.h, 1)
                    for n in range(DS):
                        a.wait_ge(PE.h, C[("cbm", c, n)])
                        a.copy(Cball.ap()[:, n * TCC:(n + 1) * TCC],
                               Pg[n % 2].ap()).then_inc(AE.h, 1)
                    for dt in range(KD):
                        a.wait_ge(PE.h, C[("dpre", c, dt)])
                        a.wait_ge(VE.h, C[("r", c, dt)])
                        if dt >= 1:
                            a.wait_ge(VE.h, C[("del", c, dt - 1)])
                        elif c >= 1:
                            a.wait_ge(VE.h, C[("del", c - 1, KD - 1)])
                        a.activation(spscr[0].ap(), Pd[dt % 2].ap(), AF.Abs,
                                     bias=bdtb.ap()[:, dt:dt + 1]).then_inc(AE.h, 1)
                        a.wait_ge(AE.h, C[("sp1", c, dt)])
                        a.activation(spscr[1].ap(), spscr[0].ap(), AF.Exp,
                                     scale=-1.0).then_inc(AE.h, 1)
                        a.wait_ge(AE.h, C[("sp2", c, dt)])
                        a.activation(spscr[0].ap(), spscr[1].ap(), AF.Ln,
                                     bias=1.0).then_inc(AE.h, 1)
                    for dt in range(KD):
                        for n in range(DS):
                            lin = dt * DS + n
                            if lin == 0:
                                a.wait_ge(VE.h, C[("del", c, KD - 1)])
                            if lin >= 2:
                                pdt, pn = divmod(lin - 2, DS)
                                a.wait_ge(VE.h, C[("scan", c, pdt, pn)])
                            elif c >= 1:
                                pdt, pn = divmod(KD * DS - 2 + lin, DS)
                                a.wait_ge(VE.h, C[("scan", c - 1, pdt, pn)])
                            a.activation(abuf[lin % 2].ap(),
                                         delt.ap()[:, dt * TCC:(dt + 1) * TCC],
                                         AF.Exp,
                                         scale=Ab.ap()[:, dt * DS + n:
                                                       dt * DS + n + 1]
                                         ).then_inc(AE.h, 1)
            S.emit("act", c_act)

            def c_pe(t):
                for c in range(NCH):
                    if c >= 1:
                        t.wait_ge(VE.h, C[("Cev", c - 1)])
                    for kc in range(KD):
                        t.matmul(PbcB.ap(),
                                 WxTb.ap()[:, kc * 80 + RK:kc * 80 + 80],
                                 xnT.ap()[:, kc * T + c * TCC:kc * T + (c + 1) * TCC],
                                 start=(kc == 0), stop=(kc == KD - 1)
                                 ).then_inc(PE.h, 1)
                    for kc in range(KD):
                        t.matmul(PbcC.ap(),
                                 WCT.ap()[:, kc * DS:(kc + 1) * DS],
                                 xnT.ap()[:, kc * T + c * TCC:kc * T + (c + 1) * TCC],
                                 start=(kc == 0), stop=(kc == KD - 1)
                                 ).then_inc(PE.h, 1)
                    if c >= 1:
                        t.wait_ge(AE.h, C[("sp1", c - 1, KD - 2)])
                        t.wait_ge(VE.h, C[("r", c - 1, KD - 2)])
                        t.wait_ge(AE.h, C[("dlev", c - 1)])
                    for kc in range(KD):
                        t.matmul(Pd[0].ap()[0:RK, :],
                                 WxTb.ap()[:, kc * 80:kc * 80 + RK],
                                 xnT.ap()[:, kc * T + c * TCC:kc * T + (c + 1) * TCC],
                                 start=(kc == 0), stop=(kc == KD - 1)
                                 ).then_inc(PE.h, 1)
                    t.wait_ge(VE.h, C[("Bev", c)])
                    for n in range(DS):
                        if n >= 2:
                            t.wait_ge(AE.h, C[("bbe", c, n - 2)])
                        elif c >= 1:
                            t.wait_ge(AE.h, C[("cbe", c - 1, DS - 2 + n)])
                        t.matmul(Pg[n % 2].ap(), e16b.ap()[:, n * P:(n + 1) * P],
                                 Btf.ap(), start=True, stop=True).then_inc(PE.h, 1)
                    t.wait_ge(VE.h, C[("Cev", c)])
                    for n in range(DS):
                        if n >= 2:
                            t.wait_ge(AE.h, C[("cbe", c, n - 2)])
                        else:
                            t.wait_ge(AE.h, C[("bbe", c, DS - 2 + n)])
                        t.matmul(Pg[n % 2].ap(), e16b.ap()[:, n * P:(n + 1) * P],
                                 Ctf.ap(), start=True, stop=True).then_inc(PE.h, 1)
                    t.wait_ge(AE.h, C[("dlev", c)])
                    for dt in range(KD):
                        if dt >= 2:
                            t.wait_ge(AE.h, C[("sp1", c, dt - 2)])
                            t.wait_ge(VE.h, C[("r", c, dt - 2)])
                        elif c >= 1 and dt == 1:
                            t.wait_ge(AE.h, C[("sp1", c - 1, KD - 1)])
                            t.wait_ge(VE.h, C[("r", c - 1, KD - 1)])
                        t.matmul(Pd[dt % 2].ap(), WdtT.ap()[:, dt * P:(dt + 1) * P],
                                 dlowb.ap(), start=True, stop=True).then_inc(PE.h, 1)
                    for dt in range(KD):
                        t.wait_ge(VE.h, C[("sg", c, dt)])
                        for tau in range(NTC):
                            lin = dt * NTC + tau
                            if lin >= 2:
                                pdt, ptau = divmod(lin - 2, NTC)
                                t.wait_ge(VE.h, C[("add", c, pdt, ptau)])
                            elif c >= 1:
                                t.wait_ge(VE.h,
                                          C[("add", c - 1, KD - 1, NTC - 2 + lin)])
                            t.transpose(Ptf2[lin % 2].ap(),
                                        sgf[dt % 2].ap()[:, tau * P:(tau + 1) * P],
                                        idf.ap()).then_inc(PE.h, 1)
            S.emit("pe", c_pe)

            def c_dve(v):
                for c in range(NCH):
                    v.wait_ge(PE.h, C[("bcB", c)])
                    v.tensor_copy(Btf.ap(), PbcB.ap()).then_inc(VE.h, 1)
                    v.wait_ge(PE.h, C[("bcC", c)])
                    v.tensor_copy(Ctf.ap(), PbcC.ap()).then_inc(VE.h, 1)
                    for dt in range(KD):
                        dsl = delt.ap()[:, dt * TCC:(dt + 1) * TCC]
                        v.wait_ge(PE.h, C[("dpre", c, dt)])
                        v.tensor_scalar(dsl, Pd[dt % 2].ap(),
                                        bdtb.ap()[:, dt:dt + 1], 0.0,
                                        ALU.add, ALU.max).then_inc(VE.h, 1)
                        v.wait_ge(AE.h, C[("sp3", c, dt)])
                        v.wait_ge(VE.h, C[("r", c, dt)])
                        v.tensor_tensor(dsl, dsl, spscr[0].ap(), ALU.add
                                        ).then_inc(VE.h, 1)
                    for dt in range(KD):
                        bl = balls[dt % 2].ap()
                        v.wait_ge(VE.h, C[("del", c, dt)])
                        v.tensor_tensor(ub1.ap(),
                                        delt.ap()[:, dt * TCC:(dt + 1) * TCC],
                                        xnT.ap()[:, dt * T + c * TCC:
                                                 dt * T + (c + 1) * TCC],
                                        ALU.mult).then_inc(VE.h, 1)
                        v.wait_ge(AE.h, C[("bbe", c, DS - 1)])
                        v.wait_ge(VE.h, C[("u", c, dt)])
                        if dt >= 2:
                            v.wait_ge(VE.h, C[("scan", c, dt - 2, DS - 1)])
                        elif c >= 1:
                            v.wait_ge(VE.h, C[("scan", c - 1, KD - 2 + dt, DS - 1)])
                        v.tensor_tensor(
                            bl.rearrange("p (n t) -> p n t", n=DS),
                            Bball.ap().rearrange("p (n t) -> p n t", n=DS),
                            ub1.ap().unsqueeze(1).broadcast_to([P, DS, TCC]),
                            ALU.mult).then_inc(VE.h, 1)
                        for n in range(DS):
                            v.wait_ge(VE.h, C[("ball", c, dt)])
                            if n == 0:
                                if dt >= 1:
                                    v.wait_ge(VE.h, C[("red", c, dt - 1)])
                                elif c >= 1:
                                    v.wait_ge(VE.h, C[("red", c - 1, KD - 1)])
                            v.wait_ge(AE.h, C[("a", c, dt, n)])
                            v.tensor_tensor_scan(
                                hall.ap()[:, n * TCC:(n + 1) * TCC],
                                abuf[(dt * DS + n) % 2].ap(),
                                bl[:, n * TCC:(n + 1) * TCC],
                                hcarry.ap()[:, dt * DS + n:dt * DS + n + 1],
                                ALU.mult, ALU.add).then_inc(VE.h, 1)
                        v.wait_ge(VE.h, C[("scan", c, dt, DS - 1)])
                        v.tensor_copy(
                            hcarry.ap()[:, dt * DS:(dt + 1) * DS].unsqueeze(2),
                            hall.ap().rearrange("p (n t) -> p n t", n=DS)[
                                :, :, TCC - 1:TCC]).then_inc(VE.h, 1)
                        v.wait_ge(AE.h, C[("cbe", c, DS - 1)])
                        v.wait_ge(VE.h, C[("carry", c, dt)])
                        v.tensor_tensor(hall.ap(), hall.ap(), Cball.ap(),
                                        ALU.mult).then_inc(VE.h, 1)
                        hv = lambda a_, b_, n_: (
                            hall.ap().rearrange("p (n t) -> p n t", n=DS)[:, a_:b_, :])
                        v.wait_ge(VE.h, C[("M", c, dt)])
                        v.tensor_tensor(hv(0, 8, DS), hv(0, 8, DS), hv(8, 16, DS),
                                        ALU.add).then_inc(VE.h, 1)
                        v.wait_ge(VE.h, C[("t1", c, dt)])
                        v.tensor_tensor(hv(0, 4, DS), hv(0, 4, DS), hv(4, 8, DS),
                                        ALU.add).then_inc(VE.h, 1)
                        v.wait_ge(VE.h, C[("t2", c, dt)])
                        v.tensor_tensor(hv(0, 2, DS), hv(0, 2, DS), hv(2, 4, DS),
                                        ALU.add).then_inc(VE.h, 1)
                        v.wait_ge(VE.h, C[("t3", c, dt)])
                        if dt >= 1:
                            v.wait_ge(VE.h, C[("sg", c, dt - 1)])
                        elif c >= 1:
                            v.wait_ge(VE.h, C[("sg", c - 1, KD - 1)])
                        v.tensor_tensor(yb1.ap(),
                                        hall.ap()[:, 0:TCC],
                                        hall.ap()[:, TCC:2 * TCC],
                                        ALU.add).then_inc(VE.h, 1)
                        v.wait_ge(VE.h, C[("red", c, dt)])
                        if dt >= 2:
                            v.wait_ge(PE.h, C[("sgt", c, dt - 2, NTC - 1)])
                        elif c >= 1:
                            v.wait_ge(PE.h, C[("sgt", c - 1, KD - 2 + dt, NTC - 1)])
                        v.tensor_tensor(sgf[dt % 2].ap(), yb1.ap(),
                                        gb.ap()[:, dt * T + c * TCC:
                                                dt * T + (c + 1) * TCC],
                                        ALU.mult).then_inc(VE.h, 1)
                        if dt == 0:
                            v.wait_ge(SPD.h, C[("x", c)])
                        for tau in range(NTC):
                            v.wait_ge(PE.h, C[("sgt", c, dt, tau)])
                            xslc = xo.ap()[:, tau * D + dt * P:tau * D + (dt + 1) * P]
                            v.tensor_tensor(xslc, Ptf2[(dt * NTC + tau) % 2].ap(),
                                            xslc, ALU.add).then_inc(VE.h, 1)
            S.emit("dve", c_dve)

            if DEBUG:
                DBG.bump(32)

                def dbg_c(g):
                    g.wait_ge(VE.h, C[("u", 0, KD - 1)])
                    g.dma_start(ddel_d, delt.ap()).then_inc(DBG.h, 16)
                    g.wait_ge(VE.h, C[("sg", 0, 0)])
                    g.dma_start(dy_d, yb1.ap()).then_inc(DBG.h, 16)
                    g.wait_ge(DBG.h, 80)
                S.emit("pool", dbg_c)

            final_out = OUT.v

            def c_fin(s):
                s.wait_ge(OUT.h, final_out)
            S.emit("sp", c_fin)

            S.run_block(nc)

    return nc


def _consts():
    iota = np.broadcast_to(np.arange(NE, dtype=np.float32), (P, NE)).copy()
    e16 = np.zeros((NE, NE * P), np.float32)
    for n in range(NE):
        e16[n, n * P:(n + 1) * P] = 1.0
    return {
        "identf": np.eye(P, dtype=np.float32),
        "identb": np.eye(P, dtype=ml_dtypes.bfloat16),
        "iota16": iota,
        "iota100": iota + 100.0,
        "e16": e16,
    }


def kernel(L0_attn_x, L0_attn_A_log, L0_attn_W_x, L0_attn_W_dt, L0_attn_b_dt,
           L0_attn_W_C, L0_ffn_router_w, L0_ffn_router_bias, L0_ffn_up_proj,
           L0_ffn_gate_proj, L0_ffn_down_proj, L0_ffn_sh_gate, L0_ffn_sh_up,
           L0_ffn_sh_down):
    x = np.ascontiguousarray(np.asarray(L0_attn_x, dtype=np.float32))
    shared = {
        "A_log": np.ascontiguousarray(L0_attn_A_log, dtype=np.float32),
        "W_x": np.ascontiguousarray(L0_attn_W_x, dtype=np.float32),
        "W_dt": np.ascontiguousarray(L0_attn_W_dt, dtype=np.float32),
        "b_dt": np.ascontiguousarray(L0_attn_b_dt, dtype=np.float32),
        "W_C": np.ascontiguousarray(L0_attn_W_C, dtype=np.float32),
        "router_w": np.ascontiguousarray(L0_ffn_router_w, dtype=np.float32),
        "router_bias": np.ascontiguousarray(L0_ffn_router_bias, dtype=np.float32),
        "up_proj": np.ascontiguousarray(L0_ffn_up_proj, dtype=np.float32),
        "gate_proj": np.ascontiguousarray(L0_ffn_gate_proj, dtype=np.float32),
        "down_proj": np.ascontiguousarray(L0_ffn_down_proj, dtype=np.float32),
        "sh_gate": np.ascontiguousarray(L0_ffn_sh_gate, dtype=np.float32),
        "sh_up": np.ascontiguousarray(L0_ffn_sh_up, dtype=np.float32),
        "sh_down": np.ascontiguousarray(L0_ffn_sh_down, dtype=np.float32),
        **_consts(),
    }
    nc = build_nc()
    in_maps = [dict(shared, x=np.ascontiguousarray(x[c])) for c in range(8)]
    res = run_bass_kernel_spmd(nc, in_maps, core_ids=list(range(8)))
    out = np.stack([res.results[c]["out"] for c in range(8)], axis=0)
    return out.astype(np.float32)
